# version 06
# speedup vs baseline: 1.6467x; 1.0758x over previous
import numpy as np
import ml_dtypes
import concourse.bass as bass
import concourse.mybir as mybir
from concourse.alu_op_type import AluOpType as OP
from concourse.bass_utils import run_bass_kernel_spmd
from contextlib import ExitStack

F32 = mybir.dt.float32
BF16 = mybir.dt.bfloat16
I32 = mybir.dt.int32
AF = mybir.ActivationFunctionType
AX = mybir.AxisListType
bf16 = ml_dtypes.bfloat16

NCORES = 8
D = 4096
TOK = 1024
TH = 512
NCH = D // 128
Bsz, S = 2, 4096
EPS = 1e-5
DBG_SERIAL = False
DBG_DUMP = False


def tile_po(w_do):
    d, o = w_do.shape
    na, nj = d // 128, o // 128
    t = w_do.reshape(na, 128, nj, 128).transpose(2, 1, 0, 3)
    return np.ascontiguousarray(t.reshape(nj, 128, na * 128))



class Sem:
    __slots__ = ("h", "n")
    def __init__(self, stack, nc, name):
        self.h = stack.enter_context(nc.semaphore(name))
        self.n = 0


class Ctx:
    ENGS = ("sync", "gpsimd", "tensor", "vector", "scalar")

    def __init__(self, nc, stack):
        self.nc = nc
        self.stack = stack
        self.plan = {e: [] for e in self.ENGS}
        self.sems = {}

    def sem(self, name):
        if name not in self.sems:
            self.sems[name] = Sem(self.stack, self.nc, name)
        return self.sems[name]

    def emit(self, eng, fn, sem=None, dn=1):
        if sem is None:
            self.plan[eng].append(fn)
            return None
        sem.n += dn
        h, d = sem.h, dn
        def g(e, fn=fn, h=h, d=d):
            inst = fn(e)
            inst.then_inc(h, d)
        self.plan[eng].append(g)
        return sem.n

    def dma(self, eng, out, in_, sem):
        return self.emit(eng, lambda e, o=out, i=in_: e.dma_start(out=o, in_=i), sem, 16)

    def wait(self, eng, sem, val=None):
        v = sem.n if val is None else val
        assert v <= sem.n, f"wait target {v} > emitted {sem.n}"
        self.plan[eng].append(lambda e, h=sem.h, v=v: e.wait_ge(h, v))

    def run_block(self):
        nc = self.nc
        with nc.Block() as block:
            @block.sync
            def _(e):
                for f in self.plan["sync"]:
                    f(e)
            @block.gpsimd
            def _(e):
                for f in self.plan["gpsimd"]:
                    f(e)
            @block.tensor
            def _(e):
                for f in self.plan["tensor"]:
                    f(e)
            @block.vector
            def _(e):
                for f in self.plan["vector"]:
                    f(e)
            @block.scalar
            def _(e):
                for f in self.plan["scalar"]:
                    f(e)



def build_program(n_sub=6):
    nc = bass.Bass("TRN2", target_bir_lowering=False)
    stack = ExitStack()
    ctx = Ctx(nc, stack)
    din = {}
    def DI(name, shape, dtype=F32):
        din[name] = nc.dram_tensor(name, list(shape), dtype, kind="ExternalInput").ap()
        return din[name]

    x0 = DI("x_shard", [D, TOK])
    y_out = nc.dram_tensor("y_shard", [D, TOK], F32, kind="ExternalOutput").ap()
    xa = nc.dram_tensor("x_ping", [D, TOK], F32).ap()
    xb = nc.dram_tensor("x_pong", [D, TOK], F32).ap()

    DI("wb_t", [NCH, 128, D])
    DI("acat_t", [128, 128 * NCH])
    DI("bcat", [128, D])
    DI("rwt", [128, 16 * NCH])
    DI("b_cols0", [128, NCH])
    DI("rb_row", [1, 16])
    DI("prior_col", [16, 1])
    DI("active_row", [1, 4], I32)
    DI("active_col", [4, 1], I32)
    DI("iota_e8_4", [4, 128])
    DI("iota16_col", [16, 1])
    DI("wq_perm2", [32, 128, D])
    DI("wk_t", [128, D])
    DI("wv_t", [128, D])
    DI("bq_nat", [128, 32])
    DI("bk_col", [128, 1])
    DI("bv_col", [128, 1])
    DI("ident128", [128, 128])
    DI("expand_full", [128, D])

    def A(name, shape, dtype, kind="sbuf"):
        f = nc.sbuf_tensor if kind == "sbuf" else nc.psum_tensor
        return stack.enter_context(f(name, list(shape), dtype))

    ps_main = A("ps_main", [128, 1024], F32, "psum")
    ps_aux = A("ps_aux", [128, 1024], F32, "psum")
    ps_bc = A("ps_bc", [128, 1024], F32, "psum")
    ps_rows = A("ps_rows", [128, 1024], F32, "psum")

    ARENA_B = 65536 + 54272
    AR = A("arena", [128, ARENA_B // 2], BF16)
    ARF = AR[:].bitcast(F32)

    U32 = mybir.dt.uint32
    ARU = AR[:].bitcast(U32)

    def av(off, cols, dtype=BF16, part=128):
        if dtype == BF16:
            assert off % 2 == 0
            return AR[0:part, off // 2: off // 2 + cols]
        assert off % 4 == 0
        src = ARU if dtype == U32 else ARF
        return src[0:part, off // 4: off // 4 + cols]

    L0R = 65536
    Xs = av(0, NCH * TOK)
    r_m_t = A("r_m", [1, TOK], F32)
    r_m = r_m_t[0:1, :]
    r_rs = A("r_rs", [1, TOK], F32)
    r_tmp = A("r_tmp", [1, TOK], F32)
    rs_bc = A("rs_bc", [128, TOK], F32)
    m_bc = A("m_bc", [128, TOK], F32)
    xsc = A("xsc", [128, TOK], F32)
    ones_col_bf = A("ones_col_bf", [128, 1], BF16)
    ones_row_bf = A("ones_row_bf", [1, TH], BF16)
    ones_row_f = A("ones_row_f", [1, 128], F32)
    twos_row_f = A("twos_row_f", [1, 4], F32)
    onecol4_f = A("onecol4_f", [4, 1], F32)
    eps_col = A("eps_col", [1, 1], F32)
    eps12_col = A("eps12_col", [4, 1], F32)
    xf = [A(f"xf{i}", [128, TOK], F32) for i in range(2)]
    cbu = [A(f"cbu{i}", [128, TOK], BF16) for i in range(2)]
    sqb = [A(f"sqb{i}", [128, TOK], BF16) for i in range(2)]
    evb = [A(f"evb{i}", [128, TH], F32) for i in range(2)]

    s_xf = [ctx.sem("s_xf0"), ctx.sem("s_xf1")]
    s_stp = [ctx.sem("s_st0"), ctx.sem("s_st1")]
    s_wb = [ctx.sem("s_wb0"), ctx.sem("s_wb1")]
    s_sm = ctx.sem("s_sm")
    s_gm = ctx.sem("s_gm")
    s_mm = ctx.sem("s_mm")
    s_v = ctx.sem("s_v")
    s_sc = ctx.sem("s_sc")

    _consts = [(ones_col_bf, 1.0), (ones_row_bf, 1.0), (ones_row_f, 1.0),
               (twos_row_f, 2.0), (onecol4_f, 1.0), (eps_col, EPS), (eps12_col, 1e-12)]
    for t, val in _consts[:-1]:
        ctx.emit("vector", lambda e, t=t, val=val: e.memset(t[:], val))
    ctx.emit("vector", lambda e: e.memset(_consts[-1][0][:], _consts[-1][1]), s_v)

    def stats_mms(src_bf, src_sq, th_list, first, last_flag):
        def f(t, src_bf=src_bf, src_sq=src_sq, th_list=tuple(th_list), first=first,
              last_flag=last_flag):
            last = None
            for th in th_list:
                sl = slice(th * TH, (th + 1) * TH)
                t.matmul(ps_rows[0:1, sl], ones_col_bf[:], src_bf[:, sl],
                         start=first, stop=last_flag, skip_group_check=True)
                last = t.matmul(ps_rows[32:33, sl], ones_col_bf[:], src_sq[:, sl],
                                start=first, stop=last_flag, skip_group_check=True)
            return last
        return ctx.emit("tensor", f, s_mm)

    def stats_initial(xsrc):
        cast_ms = {}
        stat_ms = {}
        for a in range(NCH):
            b = a % 2
            if a >= 2:
                ctx.wait("sync", s_sc, cast_ms[a - 2])
            ctx.dma("sync", xf[b][:], xsrc[a * 128:(a + 1) * 128, :], s_xf[b])
            if a >= 2:
                ctx.wait("scalar", s_mm, stat_ms[a - 2])
            ctx.wait("scalar", s_xf[b])
            ctx.emit("scalar", lambda e, b=b: e.activation(out=cbu[b][:], in_=xf[b][:], func=AF.Identity), s_sc)
            cast_ms[a] = ctx.emit("scalar", lambda e, b=b: e.activation(out=sqb[b][:], in_=xf[b][:], func=AF.Square), s_sc)
            if a == 0:
                ctx.wait("tensor", s_v)
            ctx.wait("tensor", s_sc)
            stat_ms[a] = stats_mms(cbu[b], sqb[b], (0, 1), a == 0, a == NCH - 1)

    def ln_finalize():
        ctx.wait("vector", s_mm)
        ctx.emit("vector", lambda e: e.tensor_scalar(r_m, ps_rows[0:1, :], 1.0 / D, None, OP.mult), s_v)
        ctx.emit("vector", lambda e: e.tensor_scalar(r_tmp[:], ps_rows[32:33, :], 1.0 / D, None, OP.mult), s_v)
        ctx.wait("vector", s_v)
        ctx.emit("vector", lambda e: e.scalar_tensor_tensor(r_rs[:], r_m, -1.0, r_m, OP.mult, OP.mult), s_v)
        ctx.wait("vector", s_v)
        ctx.emit("vector", lambda e: e.tensor_tensor(r_rs[:], r_tmp[:], r_rs[:], OP.add), s_v)
        ctx.wait("scalar", s_v)
        ctx.emit("scalar", lambda e: e.activation(out=r_tmp[:], in_=r_rs[:], func=AF.Ln, bias=eps_col[:]), s_sc)
        ctx.wait("scalar", s_sc)
        ctx.emit("scalar", lambda e: e.activation(out=r_rs[:], in_=r_tmp[:], func=AF.Exp, scale=-0.5), s_sc)
        ctx.wait("tensor", s_sc)
        def ft(e):
            for th in range(2):
                sl = slice(th * TH, (th + 1) * TH)
                e.matmul(ps_bc[:, sl], ones_row_f[:], r_rs[0:1, sl], start=True, stop=True,
                         skip_group_check=True)
                last = e.matmul(ps_main[:, sl], ones_row_f[:], r_m[0:1, sl], start=True,
                                stop=True, skip_group_check=True)
            return last
        ctx.emit("tensor", ft, s_mm)
        ctx.wait("scalar", s_mm)
        def fbe(e):
            e.copy(out=rs_bc[:], in_=ps_bc[:, 0:TOK])
            return e.copy(out=m_bc[:], in_=ps_main[:, 0:TOK])
        ctx.emit("scalar", fbe, s_sc)

    def apply_pass(xsrc):
        tt_ms = {}
        ctx.wait("sync", s_stp[0])
        ctx.wait("sync", s_stp[1])
        for a in range(NCH):
            b = a % 2
            if a >= 2:
                ctx.wait("sync", s_v, tt_ms[a - 2])
            else:
                ctx.wait("sync", s_sc)
                ctx.wait("sync", s_v)
            ctx.dma("sync", xf[b][:], xsrc[a * 128:(a + 1) * 128, :], s_xf[b])
            ctx.wait("vector", s_xf[b])
            if a == 0:
                ctx.wait("vector", s_sc)
            ctx.wait("vector", s_v)
            ctx.emit("vector", lambda e, b=b: e.scalar_tensor_tensor(
                xsc[:], m_bc[:], -1.0, xf[b][:], OP.mult, OP.add), s_v)
            ctx.wait("vector", s_v)
            tt_ms[a] = ctx.emit("vector", lambda e, a=a: e.tensor_tensor(
                Xs[:, a * TOK:(a + 1) * TOK], xsc[:], rs_bc[:], OP.mult), s_v)

    res_state = {}
    def residual_begin(period=2):
        res_state.clear()
        res_state["n"] = 0
        res_state["period"] = period

    def residual_mmwait():
        n = res_state["n"]
        p = res_state["period"]
        if n >= p:
            ctx.wait("tensor", s_sc, res_state[("evac", n - p)])
        elif n == 0:
            ctx.wait("tensor", s_sc)

    def _emit_stats(pn):
        pb, pth, pcast = res_state[("stat", pn)]
        ctx.wait("tensor", s_sc, pcast)
        stats_mms(cbu[pb], sqb[pb], (pth,), pn <= 1, pn >= 2 * NCH - 2)

    def residual_chunk(j, th, psum_ap, scale, xold, xnew, do_stats, bias_col=None):
        n = res_state["n"]
        res_state["n"] = n + 1
        b = n % 2
        sl = slice(th * TH, (th + 1) * TH)
        mm_ms = s_mm.n
        if do_stats and n >= 1:
            _emit_stats(n - 1)
        if n >= 2:
            if do_stats:
                ctx.wait("sync", s_sc, res_state[("done", n - 2)])
            ctx.wait("sync", s_stp[b])
        else:
            ctx.wait("sync", s_sc)
            ctx.wait("sync", s_v)
        ctx.dma("sync", xf[b][:, sl], xold[j * 128:(j + 1) * 128, sl], s_xf[b])
        ctx.wait("scalar", s_mm, mm_ms)
        if n >= 2:
            ctx.wait("scalar", s_v, res_state[("add", n - 2)])
        if bias_col is not None:
            ctx.emit("scalar", lambda e, p=psum_ap, b=b, sc=scale, bc=bias_col:
                     e.activation(out=evb[b][:], in_=p, func=AF.Identity, scale=sc, bias=bc), s_sc)
        else:
            ctx.emit("scalar", lambda e, p=psum_ap, b=b, sc=scale:
                     e.activation(out=evb[b][:], in_=p, func=AF.Identity, scale=sc), s_sc)
        res_state[("evac", n)] = s_sc.n
        ctx.wait("vector", s_sc)
        ctx.wait("vector", s_xf[b])
        add_ms = ctx.emit("vector", lambda e, b=b, sl=sl: e.tensor_tensor(
            xf[b][:, sl], xf[b][:, sl], evb[b][:], OP.add), s_v)
        res_state[("add", n)] = add_ms
        ctx.wait("sync", s_v, add_ms)
        st_ms = ctx.dma("sync", xnew[j * 128:(j + 1) * 128, sl], xf[b][:, sl], s_stp[b])
        if do_stats:
            ctx.wait("scalar", s_v, add_ms)
            ctx.emit("scalar", lambda e, b=b, sl=sl: e.activation(out=cbu[b][:, sl], in_=xf[b][:, sl], func=AF.Identity), s_sc)
            cast_ms = ctx.emit("scalar", lambda e, b=b, sl=sl: e.activation(out=sqb[b][:, sl], in_=xf[b][:, sl], func=AF.Square), s_sc)
            res_state[("stat", n)] = (b, th, cast_ms)
            res_state[("done", n)] = cast_ms
        else:
            res_state[("done", n)] = st_ms

    def residual_end(do_stats):
        if do_stats:
            _emit_stats(res_state["n"] - 1)

    wbuf = [A(f"wbuf{i}", [128, D], BF16) for i in range(2)]
    acat_sb = av(L0R + 0, 128 * NCH)
    bcat_sb = av(L0R + 8192, D)
    rw_sb = av(L0R + 16384, 16 * NCH)
    b_cols0 = A("sb_b_cols0", [128, NCH], F32)
    rb_bf = A("rb_bf", [1, 16], BF16)
    s16_bf = A("s16_bf", [16, 4], BF16)
    s16_f = A("s16_f", [16, 4], F32)
    sexp_bf = A("sexp_bf", [4, 128], BF16)
    sexp_f = A("sexp_f", [4, 128], F32)
    zsig = A("zsig", [128, TOK], BF16)
    sig_sb = A("sig_sb", [128, TOK], BF16)
    alpha_bf = av(L0R + 21504, TOK, BF16, part=4)
    e4_sb = av(L0R + 17408, TOK, F32, part=4)
    lg_sb = av(L0R + 23552, TOK, BF16, part=16)
    lp_col = A("lp_col", [4, 1], F32)
    act_fr = A("act_fr", [1, 4], F32)
    act_ir = A("act_ir", [1, 4], I32)
    act_fc = A("act_fc", [4, 1], F32)
    act_ic = A("act_ic", [4, 1], I32)
    io16 = A("io16", [16, 1], F32)
    io84 = A("io84", [4, 128], F32)
    pr16 = A("pr16", [16, 1], F32)

    l0_state = {"setup": False}

    def l0ffn_setup_once():
        ctx.dma("gpsimd", acat_sb, din["acat_t"], s_gm)
        ctx.dma("gpsimd", bcat_sb, din["bcat"], s_gm)
        ctx.dma("gpsimd", rw_sb, din["rwt"], s_gm)
        ctx.dma("sync", b_cols0[:], din["b_cols0"], s_sm)
        ctx.dma("gpsimd", rb_bf[:], din["rb_row"], s_gm)
        ctx.dma("sync", act_ir[:], din["active_row"], s_sm)
        ctx.dma("sync", act_ic[:], din["active_col"], s_sm)
        ctx.dma("sync", io16[:], din["iota16_col"], s_sm)
        ctx.dma("sync", io84[:], din["iota_e8_4"], s_sm)
        ctx.dma("sync", pr16[:], din["prior_col"], s_sm)
        ctx.wait("vector", s_sm)
        ctx.emit("vector", lambda e: e.tensor_scalar(b_cols0[:], b_cols0[:], 0.5, None, OP.mult), s_v)
        def fv(e):
            e.tensor_copy(act_fr[:], act_ir[:])
            return e.tensor_copy(act_fc[:], act_ic[:])
        ctx.emit("vector", fv, s_v)
        ctx.wait("tensor", s_v)
        ctx.emit("tensor", lambda e: e.matmul(ps_rows[0:16, 0:4], ones_row_f[0:1, 0:16],
                                              act_fr[:], start=True, stop=True,
                                              skip_group_check=True), s_mm)
        ctx.wait("vector", s_mm)
        ctx.emit("vector", lambda e: e.tensor_tensor(s16_f[:], io16[:].broadcast_to([16, 4]),
                                                     ps_rows[0:16, 0:4], OP.is_equal), s_v)
        ctx.emit("vector", lambda e: e.tensor_tensor(sexp_f[:], act_fc[:].broadcast_to([4, 128]),
                                                     io84[:], OP.is_equal), s_v)
        ctx.wait("vector", s_v)
        ctx.emit("vector", lambda e: e.tensor_copy(s16_bf[:], s16_f[:]), s_v)
        ctx.emit("vector", lambda e: e.tensor_copy(sexp_bf[:], sexp_f[:]), s_v)
        ctx.wait("tensor", s_v)
        ctx.emit("tensor", lambda e: e.matmul(ps_rows[32:36, 8:9], s16_f[:], pr16[:],
                                              start=True, stop=True,
                                              skip_group_check=True), s_mm)
        ctx.wait("scalar", s_mm)
        ctx.emit("scalar", lambda e: e.activation(out=lp_col[:], in_=ps_rows[32:36, 8:9],
                                                  func=AF.Ln, bias=eps12_col[:]), s_sc)
        l0_state["setup"] = True

    def l0ffn_router_and_z():
        ctx.wait("tensor", s_v)
        ctx.wait("tensor", s_sc)
        ctx.wait("tensor", s_gm)
        def ft(e):
            for th in range(2):
                sl = slice(th * TH, (th + 1) * TH)
                for a in range(NCH):
                    e.matmul(ps_rows[0:16, sl], rw_sb[:, a * 16:(a + 1) * 16],
                             Xs[:, a * TOK + th * TH: a * TOK + (th + 1) * TH],
                             start=(a == 0), stop=False, skip_group_check=True)
                last = e.matmul(ps_rows[0:16, sl], rb_bf[:], ones_row_bf[0:1, 0:TH],
                                start=False, stop=True, skip_group_check=True)
            return last
        ctx.emit("tensor", ft, s_mm)
        ctx.wait("scalar", s_mm)
        ctx.emit("scalar", lambda e: e.copy(out=lg_sb, in_=ps_rows[0:16, 0:TOK]), s_sc)
        def ftz(e):
            for th in range(2):
                sl = slice(th * TH, (th + 1) * TH)
                for a in range(NCH):
                    last = e.matmul(ps_aux[:, sl], acat_sb[:, a * 128:(a + 1) * 128],
                                    Xs[:, a * TOK + th * TH: a * TOK + (th + 1) * TH],
                                    start=(a == 0), stop=(a == NCH - 1), skip_group_check=True)
            return last
        ctx.emit("tensor", ftz, s_mm)
        ctx.wait("tensor", s_sc)
        def fts(e):
            for th in range(2):
                sl = slice(th * TH, (th + 1) * TH)
                last = e.matmul(ps_rows[32:36, sl], s16_bf[:], lg_sb[:, sl],
                                start=True, stop=True, skip_group_check=True)
            return last
        ctx.emit("tensor", fts, s_mm)
        ctx.wait("scalar", s_mm)
        ctx.wait("scalar", s_sc)
        ctx.emit("scalar", lambda e: e.activation(out=e4_sb, in_=ps_rows[32:36, 0:TOK],
                                                  func=AF.Exp, bias=lp_col[:]), s_sc)
        ctx.wait("tensor", s_sc)
        def ftd(e):
            for th in range(2):
                sl = slice(th * TH, (th + 1) * TH)
                last = e.matmul(ps_rows[64:65, sl], onecol4_f[:], e4_sb[:, sl],
                                start=True, stop=True, skip_group_check=True)
            return last
        ctx.emit("tensor", ftd, s_mm)
        ctx.wait("vector", s_mm)
        ctx.wait("vector", s_v)
        ctx.emit("vector", lambda e: e.reciprocal(r_tmp[:], ps_rows[64:65, 0:TOK]), s_v)
        ctx.wait("tensor", s_v)
        def ftb(e):
            for th in range(2):
                sl = slice(th * TH, (th + 1) * TH)
                last = e.matmul(ps_bc[0:4, sl], twos_row_f[:], r_tmp[0:1, sl],
                                start=True, stop=True, skip_group_check=True)
            return last
        ctx.emit("tensor", ftb, s_mm)
        ctx.wait("vector", s_mm)
        ctx.emit("vector", lambda e: e.tensor_tensor(alpha_bf, e4_sb,
                                                     ps_bc[0:4, 0:TOK], OP.mult), s_v)
        ctx.wait("tensor", s_v)
        def ftsg(e):
            for th in range(2):
                sl = slice(th * TH, (th + 1) * TH)
                last = e.matmul(ps_bc[:, sl], sexp_bf[:], alpha_bf[:, sl],
                                start=True, stop=True)
            return last
        ctx.emit("tensor", ftsg, s_mm)
        ctx.wait("scalar", s_mm)
        ctx.emit("scalar", lambda e: e.copy(out=sig_sb[:], in_=ps_bc[:, 0:TOK]), s_sc)
        ctx.wait("vector", s_sc)
        ctx.emit("vector", lambda e: e.tensor_tensor(zsig[:], ps_aux[:, 0:TOK], sig_sb[:],
                                                     OP.mult), s_v)

    def l0ffn(xold, xnew, do_stats):
        if not l0_state["setup"]:
            l0ffn_setup_once()
        l0ffn_router_and_z()
        ctx.wait("scalar", s_v)
        ctx.wait("gpsimd", s_mm)
        residual_begin(4)
        wb_ms = {}
        for j in range(NCH):
            b = j % 2
            pj = ps_main if j % 2 == 0 else ps_aux
            if j >= 2:
                ctx.wait("gpsimd", s_mm, wb_ms[j - 2])
            ctx.dma("gpsimd", wbuf[b][:], din["wb_t"][j], s_wb[b])
            ctx.wait("tensor", s_wb[b])
            if j == 0:
                ctx.wait("tensor", s_v)
            for th in range(2):
                psl = slice(th * TH, (th + 1) * TH)
                residual_mmwait()
                def ftm(e, b=b, th=th, psl=psl, j=j, pj=pj):
                    sl = slice(th * TH, (th + 1) * TH)
                    for a in range(NCH):
                        e.matmul(pj[:, psl], wbuf[b][:, a * 128:(a + 1) * 128],
                                 Xs[:, a * TOK + th * TH: a * TOK + (th + 1) * TH],
                                 start=(a == 0), stop=False)
                    return e.matmul(pj[:, psl], bcat_sb[:, j * 128:(j + 1) * 128],
                                    zsig[:, sl], start=False, stop=True)
                wb_ms[j] = ctx.emit("tensor", ftm, s_mm)
                residual_chunk(j, th, pj[:, psl], 0.5, xold, xnew, do_stats,
                               bias_col=b_cols0[:, j:j + 1])
        residual_end(do_stats)

    wqld = [None, None]
    wqs_sb = av(L0R + 17408, D)
    wk_sb = av(L0R + 25600, D)
    wv_sb = av(L0R + 33792, D)
    ident_bf = A("ident_bf", [128, 128], BF16)
    expand_sb = av(L0R + 41984, D)
    bqs_col = A("bqs_col", [128, 1], F32)
    bq_nat = A("sb_bq_nat", [128, 32], F32)
    bk_col = A("sb_bk_col", [128, 1], F32)
    bv_col = A("sb_bv_col", [128, 1], F32)
    qs_sb = cbu[0]
    qsT_sb = cbu[1]
    k_bf = sqb[0]
    kT_sb = sqb[1]
    fqs = [av(0, 16 * 128, BF16, part=1), av(4096, 16 * 128, BF16, part=1)]
    fks = [av(8192, 16 * 128, BF16, part=1), av(12288, 16 * 128, BF16, part=1)]
    vone = av(16384, 2 * TOK)
    v_bf = av(L0R + 50176, TOK)
    et_sb = [evb[i][:].bitcast(BF16)[:, 0:512] for i in range(2)]
    o_sb = av(L0R + 52224, TOK)
    qst_dram = nc.dram_tensor("qst_dram", [128, TOK], BF16).ap()
    kst_dram = nc.dram_tensor("kst_dram", [128, TOK], BF16).ap()

    l0a_state = {"setup": False}

    def l0attn_setup_once():
        wqld[0] = wbuf[1][:, 0:1024]
        wqld[1] = wbuf[1][:, 1024:2048]
        wqsT_bf = wbuf[0]
        ctx.dma("gpsimd", ident_bf[:], din["ident128"], s_gm)
        ctx.dma("gpsimd", expand_sb, din["expand_full"], s_gm)
        ctx.dma("gpsimd", wk_sb, din["wk_t"], s_gm)
        ctx.dma("gpsimd", wv_sb, din["wv_t"], s_gm)
        ctx.dma("sync", bq_nat[:], din["bq_nat"], s_sm)
        ctx.dma("sync", bk_col[:], din["bk_col"], s_sm)
        ctx.dma("sync", bv_col[:], din["bv_col"], s_sm)
        ctx.wait("vector", s_sm)
        ctx.emit("vector", lambda e: e.reduce_sum(bqs_col[:], bq_nat[:], axis=AX.X), s_v)
        ctx.wait("vector", s_v)
        ctx.emit("vector", lambda e: e.tensor_scalar(bqs_col[:], bqs_col[:],
                                                     float(1.0 / np.sqrt(128.0)), None, OP.mult), s_v)
        ctx.wait("gpsimd", s_wb[0])
        ctx.wait("gpsimd", s_wb[1])
        ctx.wait("gpsimd", s_mm)
        ldms = {}
        for sgrp in range(4):
            csl = slice(sgrp * 1024, (sgrp + 1) * 1024)
            for h in range(32):
                b = h % 2
                if h >= 2:
                    ctx.wait("gpsimd", s_mm, ldms[h - 2])
                ctx.dma("gpsimd", wqld[b], din["wq_perm2"][h, :, csl], s_wb[b])
                ctx.wait("tensor", s_wb[b])
                if sgrp == 0 and h == 0:
                    ctx.wait("tensor", s_gm)
                    ctx.wait("tensor", s_v)
                    ctx.wait("tensor", s_sc)
                def fti(e, b=b, h=h):
                    base = b * 1024
                    e.matmul(ps_main[:, 0:512], ident_bf[:], wbuf[1][:, base:base + 512],
                             start=(h == 0), stop=(h == 31), skip_group_check=True)
                    return e.matmul(ps_main[:, 512:1024], ident_bf[:], wbuf[1][:, base + 512:base + 1024],
                                    start=(h == 0), stop=(h == 31), skip_group_check=True)
                ldms[h] = ctx.emit("tensor", fti, s_mm)
            ctx.wait("scalar", s_mm)
            ctx.emit("scalar", lambda e, csl=csl: e.activation(
                out=wbuf[0][:, csl], in_=ps_main[:, 0:1024], func=AF.Identity,
                scale=float(1.0 / np.sqrt(128.0))), s_sc)
            ctx.wait("gpsimd", s_sc)
            ctx.wait("tensor", s_sc)
            ldms = {}
        ctx.wait("tensor", s_sc)
        for a in range(NCH):
            ctx.emit("tensor", lambda e, a=a: e.transpose(
                ps_aux[:].bitcast(BF16)[:, (a % 4) * 128:(a % 4) * 128 + 128],
                wbuf[0][:, a * 128:(a + 1) * 128], ident_bf[:]), s_mm)
            if a % 4 == 3:
                ctx.wait("scalar", s_mm)
                ctx.emit("scalar", lambda e, a=a: e.copy(
                    out=wqs_sb[:, (a - 3) * 128:(a + 1) * 128],
                    in_=ps_aux[:].bitcast(BF16)[:, 0:512]), s_sc)
                if a < NCH - 1:
                    ctx.wait("tensor", s_sc)
        ctx.wait("tensor", s_gm)
        l0a_state["setup"] = True

    def l0attn(xold, xnew, do_stats):
        if not l0a_state["setup"]:
            l0attn_setup_once()
        ctx.wait("tensor", s_v)
        ctx.wait("tensor", s_sc)
        def ftp(e):
            last = None
            for psum, w in ((ps_main, wqs_sb), (ps_aux, wk_sb), (ps_bc, wv_sb)):
                for th in range(2):
                    sl = slice(th * TH, (th + 1) * TH)
                    for a in range(NCH):
                        last = e.matmul(psum[:, sl], w[:, a * 128:(a + 1) * 128],
                                        Xs[:, a * TOK + th * TH: a * TOK + (th + 1) * TH],
                                        start=(a == 0), stop=(a == NCH - 1),
                                        skip_group_check=True)
            return last
        ctx.emit("tensor", ftp, s_mm)
        ctx.wait("scalar", s_mm)
        ctx.wait("scalar", s_v)
        def fev(e):
            e.activation(out=qs_sb[:], in_=ps_main[:, 0:TOK], func=AF.Identity, bias=bqs_col[:])
            e.activation(out=k_bf[:], in_=ps_aux[:, 0:TOK], func=AF.Identity, bias=bk_col[:])
            return e.activation(out=v_bf, in_=ps_bc[:, 0:TOK], func=AF.Identity,
                                bias=bv_col[:])
        ctx.emit("scalar", fev, s_sc)
        ctx.wait("tensor", s_sc)
        for src, dst in ((qs_sb, qsT_sb), (k_bf, kT_sb)):
            for blk in range(8):
                ctx.emit("tensor", lambda e, blk=blk, src=src: e.transpose(
                    ps_main[:].bitcast(BF16)[:, (blk % 4) * 128:(blk % 4) * 128 + 128],
                    src[:, blk * 128:(blk + 1) * 128], ident_bf[:]), s_mm)
                if blk % 4 == 3:
                    ctx.wait("scalar", s_mm)
                    ctx.emit("scalar", lambda e, blk=blk, dst=dst: e.copy(
                        out=dst[:, (blk - 3) * 128:(blk + 1) * 128],
                        in_=ps_main[:].bitcast(BF16)[:, 0:512]), s_sc)
                    ctx.wait("tensor", s_sc)
        ctx.wait("sync", s_sc)
        ctx.dma("sync", qst_dram, qsT_sb[:], s_sm)
        ctx.dma("sync", kst_dram, kT_sb[:], s_sm)
        ctx.wait("vector", s_mm)
        ctx.emit("vector", lambda e: e.memset(vone, 1.0), s_v)
        ctx.wait("vector", s_v)
        ctx.wait("vector", s_sc)
        vo_view = vone.rearrange("p (t two) -> p t two", two=2)[:, :, 0:1]\
            .rearrange("p t two -> p (t two)")
        ctx.emit("vector", lambda e: e.tensor_copy(vo_view, v_bf), s_v)
        ctx.wait("sync", s_sm)
        ctx.wait("sync", s_mm)
        ctx.wait("tensor", s_v)
        exp_ms = {}
        nd_ms = {}
        ch_ms = {}
        for g in range(TOK // 4):
            slot = g % 2
            base = slot * 512
            if g % 4 == 0:
                ch = g // 4
                fb = ch % 2
                if ch >= 2:
                    ctx.wait("sync", s_mm, ch_ms[ch - 2])
                t0 = ch * 16
                blk0, tl0 = t0 // 128, t0 % 128
                ctx.dma("sync", fqs[fb].rearrange("a (p d) -> a p d", p=16),
                        qst_dram[tl0:tl0 + 16, blk0 * 128:(blk0 + 1) * 128].unsqueeze(0),
                        s_xf[fb])
                ctx.dma("sync", fks[fb].rearrange("a (p d) -> a p d", p=16),
                        kst_dram[tl0:tl0 + 16, blk0 * 128:(blk0 + 1) * 128].unsqueeze(0),
                        s_xf[fb])
                ctx.wait("tensor", s_xf[fb])
            if g >= 2:
                ctx.wait("tensor", s_sc, exp_ms[g - 2])
            def fts(e, g=g, base=base):
                fb = (g // 4) % 2
                for i in range(4):
                    t = 4 * g + i
                    o = (t % 16) * 128
                    last = e.matmul(ps_main[:, base + i * 128: base + (i + 1) * 128],
                                    fks[fb][0:1, o:o + 128], fqs[fb][0:1, o:o + 128],
                                    start=True, stop=True, skip_group_check=True)
                return last
            sm = ctx.emit("tensor", fts, s_mm)
            ch_ms[g // 4] = sm
            ctx.wait("scalar", s_mm, sm)
            if g >= 2:
                ctx.wait("scalar", s_mm, nd_ms[g - 2])
            exp_ms[g] = ctx.emit("scalar", lambda e, slot=slot, base=base: e.activation(
                out=et_sb[slot], in_=ps_main[:, base:base + 512], func=AF.Exp), s_sc)

            def emit_nd(ga):
                ctx.wait("tensor", s_sc, exp_ms[ga])
                def ftnd(e, g=ga):
                    for i in range(4):
                        t = 4 * g + i
                        ps = ps_bc if t < 512 else ps_aux
                        c = 2 * (t % 512)
                        last = e.matmul(ps[:, c:c + 2],
                                        et_sb[g % 2][:, i * 128:(i + 1) * 128],
                                        vone[:, 2 * t:2 * t + 2], start=True, stop=True,
                                        skip_group_check=True)
                    return last
                nd_ms[ga] = ctx.emit("tensor", ftnd, s_mm)
            if g >= 1:
                emit_nd(g - 1)
        emit_nd(TOK // 4 - 1)
        ctx.wait("vector", s_mm)
        ctx.wait("vector", s_v)
        def psv(ps, which):
            return ps[:].rearrange("p (t two) -> p t two", two=2)[:, :, which:which + 1]\
                .rearrange("p t two -> p (t two)")
        def fvr(e):
            e.reciprocal(rs_bc[:, 0:TH], psv(ps_bc, 1))
            return e.reciprocal(rs_bc[:, TH:TOK], psv(ps_aux, 1))
        ctx.emit("vector", fvr, s_v)
        ctx.wait("vector", s_v)
        def fvo2(e):
            e.tensor_tensor(o_sb[:, 0:TH], psv(ps_bc, 0), rs_bc[:, 0:TH], OP.mult)
            return e.tensor_tensor(o_sb[:, TH:TOK], psv(ps_aux, 0), rs_bc[:, TH:TOK],
                                   OP.mult)
        ctx.emit("vector", fvo2, s_v)
        ctx.wait("scalar", s_sm)
        residual_begin(4)
        ctx.wait("tensor", s_v)
        for j in range(NCH):
            pj = ps_main if j % 2 == 0 else ps_aux
            for th in range(2):
                psl = slice(th * TH, (th + 1) * TH)
                residual_mmwait()
                def ftm(e, j=j, th=th, psl=psl, pj=pj):
                    return e.matmul(pj[:, psl], expand_sb[:, j * 128:(j + 1) * 128],
                                    o_sb[:, th * TH:(th + 1) * TH], start=True, stop=True,
                                    skip_group_check=True)
                ctx.emit("tensor", ftm, s_mm)
                residual_chunk(j, th, pj[:, psl], 1.0, xold, xnew, do_stats)
        residual_end(do_stats)

    DI("u1_t", [NCH, 128, 1024])
    DI("v1_t", [172, 128, 1024])
    DI("u2_t", [86, 128, 1024])
    DI("v2_t", [NCH, 128, 1024])
    DI("b1g_cols", [128, 86])
    DI("b1u_cols", [128, 86])
    DI("b2_cols", [128, NCH])
    P_sb = av(L0R, 8 * TOK)
    u1buf = [av(L0R + 16384, TOK), av(L0R + 18432, TOK)]
    vgbuf = [av(L0R + 20480, TOK), av(L0R + 22528, TOK)]
    vubuf = [av(L0R + 24576, TOK), av(L0R + 26624, TOK)]
    u2slab = av(L0R + 28672, 8 * TOK)
    actslab = av(0, 8 * TOK)
    z_f32 = av(16384, 8 * TOK, F32)
    z_bf = av(49152, 8 * TOK)
    b1g_sb = A("b1g_sb", [128, 86], F32)
    b1u_sb = A("b1u_sb", [128, 86], F32)
    b_cols1 = A("b_cols1", [128, NCH], F32)
    sg_sb = sig_sb
    s_u2 = [ctx.sem("s_u20"), ctx.sem("s_u21")]
    l1f_state = {"setup": False}

    def l1ffn(xold, xnew, do_stats):
        NG = (86 + 7) // 8
        if not l1f_state["setup"]:
            ctx.dma("sync", b1g_sb[:], din["b1g_cols"], s_sm)
            ctx.dma("sync", b1u_sb[:], din["b1u_cols"], s_sm)
            ctx.dma("sync", b_cols1[:], din["b2_cols"], s_sm)
            ctx.wait("vector", s_sm)
            ctx.emit("vector", lambda e: e.tensor_scalar(b_cols1[:], b_cols1[:], 0.5, None,
                                                         OP.mult), s_v)
            l1f_state["setup"] = True
        ctx.wait("tensor", s_v)
        ctx.wait("tensor", s_sc)
        ctx.wait("gpsimd", s_mm)
        u1_ms = {}
        for tq in range(4):
            tsl = slice(tq * 256, (tq + 1) * 256)
            for a in range(NCH):
                b = a % 2
                k = tq * NCH + a
                if k >= 2:
                    ctx.wait("gpsimd", s_mm, u1_ms[k - 2])
                ctx.dma("gpsimd", u1buf[b], din["u1_t"][a], s_wb[b])
                ctx.wait("tensor", s_wb[b])
                def ftp(e, a=a, b=b, tq=tq):
                    for rj in range(8):
                        ps = ps_main if rj < 4 else ps_aux
                        qsl = slice((rj % 4) * 256, (rj % 4) * 256 + 256)
                        st = (a == 0) and (rj % 2 == 0)
                        last = e.matmul(ps[:, qsl], u1buf[b][:, rj * 128:(rj + 1) * 128],
                                        Xs[:, a * TOK + tq * 256: a * TOK + (tq + 1) * 256],
                                        start=st, stop=(a == NCH - 1), skip_group_check=True)
                    return last
                u1_ms[k] = ctx.emit("tensor", ftp, s_mm)
            ctx.wait("scalar", s_mm)
            def fpe(e, tq=tq):
                for rj in range(8):
                    ps = ps_main if rj < 4 else ps_aux
                    qsl = slice((rj % 4) * 256, (rj % 4) * 256 + 256)
                    last = e.copy(out=P_sb[:, rj * TOK + tq * 256: rj * TOK + (tq + 1) * 256],
                                  in_=ps[:, qsl])
                return last
            ctx.emit("scalar", fpe, s_sc)
            if tq < 3:
                ctx.wait("tensor", s_sc)
        ctx.wait("tensor", s_sc)
        ctx.wait("gpsimd", s_mm)
        PG = [ps_main, ps_bc]
        PU = [ps_aux, ps_rows]
        vg_ms = {}
        zf_ms = {}
        fsg_ms = {}
        for jf in range(86):
            b = jf % 2
            slot = jf % 8
            g = jf // 8
            if jf >= 2:
                ctx.wait("gpsimd", s_mm, vg_ms[jf - 2])
            ctx.dma("gpsimd", vgbuf[b], din["v1_t"][jf], s_wb[b])
            ctx.dma("gpsimd", vubuf[b], din["v1_t"][86 + jf], s_wb[b])
            if jf >= 2:
                ctx.wait("gpsimd", s_v, zf_ms.get(max(0, g - 1), 0))
            ctx.dma("gpsimd", u2slab[:, slot * TOK:(slot + 1) * TOK], din["u2_t"][jf], s_u2[b])
            ctx.wait("tensor", s_wb[b], None)
            if jf >= 2:
                ctx.wait("tensor", s_sc, fsg_ms[jf - 2])
            if slot <= 1 and g > 0:
                ctx.wait("tensor", s_v, zf_ms[g - 1])
            def ftg(e, b=b):
                for th in range(2):
                    for rj in range(8):
                        e.matmul(PG[b][:, th * TH:(th + 1) * TH],
                                 vgbuf[b][:, rj * 128:(rj + 1) * 128],
                                 P_sb[:, rj * TOK + th * TH: rj * TOK + (th + 1) * TH],
                                 start=(rj == 0), stop=(rj == 7), skip_group_check=True)
                        last = e.matmul(PU[b][:, th * TH:(th + 1) * TH],
                                        vubuf[b][:, rj * 128:(rj + 1) * 128],
                                        P_sb[:, rj * TOK + th * TH: rj * TOK + (th + 1) * TH],
                                        start=(rj == 0), stop=(rj == 7), skip_group_check=True)
                return last
            vg_ms[jf] = ctx.emit("tensor", ftg, s_mm)
            ctx.wait("scalar", s_mm)
            ctx.wait("scalar", s_v)
            def fsg(e, jf=jf, b=b):
                e.activation(out=sg_sb[:], in_=PG[b][:, 0:TOK], func=AF.Silu,
                             bias=b1g_sb[:, jf:jf + 1])
                return e.activation(out=rs_bc[:], in_=PU[b][:, 0:TOK], func=AF.Identity,
                                    bias=b1u_sb[:, jf:jf + 1])
            fsg_ms[jf] = ctx.emit("scalar", fsg, s_sc)
            ctx.wait("vector", s_sc)
            ctx.emit("vector", lambda e, slot=slot: e.tensor_tensor(
                actslab[:, slot * TOK:(slot + 1) * TOK], sg_sb[:], rs_bc[:], OP.mult), s_v)
            if jf == 85 or slot == 7:
                nfi = slot + 1
                ctx.wait("tensor", s_v)
                ctx.wait("tensor", s_u2[0])
                ctx.wait("tensor", s_u2[1])
                flush_ms = {}
                for rj in range(8):
                    pbz = PG[rj % 2]
                    if rj >= 2:
                        ctx.wait("tensor", s_v, flush_ms[rj - 2])
                    def ftz(e, rj=rj, nfi=nfi, pbz=pbz):
                        for fi in range(nfi):
                            for th in range(2):
                                last = e.matmul(pbz[:, th * TH:(th + 1) * TH],
                                                u2slab[:, fi * TOK + rj * 128: fi * TOK + (rj + 1) * 128],
                                                actslab[:, fi * TOK + th * TH: fi * TOK + (th + 1) * TH],
                                                start=(fi == 0), stop=(fi == nfi - 1),
                                                skip_group_check=True)
                        return last
                    zm = ctx.emit("tensor", ftz, s_mm)
                    ctx.wait("vector", s_mm, zm)
                    if g == 0:
                        flush_ms[rj] = ctx.emit("vector", lambda e, rj=rj, pbz=pbz: e.tensor_copy(
                            z_f32[:, rj * TOK:(rj + 1) * TOK], pbz[:, 0:TOK]), s_v)
                    else:
                        flush_ms[rj] = ctx.emit("vector", lambda e, rj=rj, pbz=pbz: e.tensor_tensor(
                            z_f32[:, rj * TOK:(rj + 1) * TOK], z_f32[:, rj * TOK:(rj + 1) * TOK],
                            pbz[:, 0:TOK], OP.add), s_v)
                zf_ms[g] = s_v.n
        ctx.wait("scalar", s_v)
        def fzc(e):
            for rj in range(8):
                last = e.copy(out=z_bf[:, rj * TOK:(rj + 1) * TOK],
                              in_=z_f32[:, rj * TOK:(rj + 1) * TOK])
            return last
        ctx.emit("scalar", fzc, s_sc)
        ctx.wait("tensor", s_sc)
        ctx.wait("gpsimd", s_mm)
        residual_begin(4)
        v2_ms = {}
        for j in range(NCH):
            b = j % 2
            pj = ps_main if j % 2 == 0 else ps_aux
            if j >= 2:
                ctx.wait("gpsimd", s_mm, v2_ms[j - 2])
            ctx.dma("gpsimd", wbuf[b][:, 0:TOK], din["v2_t"][j], s_wb[b])
            ctx.wait("tensor", s_wb[b])
            for th in range(2):
                psl = slice(th * TH, (th + 1) * TH)
                residual_mmwait()
                def ftm(e, b=b, th=th, psl=psl, pj=pj):
                    for rj in range(8):
                        last = e.matmul(pj[:, psl], wbuf[b][:, rj * 128:(rj + 1) * 128],
                                        z_bf[:, rj * TOK + th * TH: rj * TOK + (th + 1) * TH],
                                        start=(rj == 0), stop=(rj == 7))
                    return last
                v2_ms[j] = ctx.emit("tensor", ftm, s_mm)
                residual_chunk(j, th, pj[:, psl], 0.5, xold, xnew, do_stats,
                               bias_col=b_cols1[:, j:j + 1])
        residual_end(do_stats)

    DI("wq1_t", [NCH, 128, D])
    DI("wkr_t", [4, 128, D])
    DI("wdk_t", [4, 128, D])
    DI("wupk_t", [8, 4, 128, 128])
    DI("wupv_t", [8, 4, 128, 128])
    DI("wo_t", [NCH, 128, D])
    DI("cos_q", [128, TOK])
    DI("sin_q", [128, TOK])
    DI("cos_k", [128, TOK])
    DI("sin_k", [128, TOK])
    DI("r_q", [128, 128])
    DI("r_k", [128, 128])
    DI("sel_even", [128, 128])
    DI("sel_odd", [128, 128])
    DI("m_tri", [128, 2048])
    U32 = mybir.dt.uint32
    DI("kv_idx", [128, 72], U32)
    DI("q_idx", [128, 9], U32)
    DI("ind_a", [128, 9])
    DI("ind_b", [128, 9])
    DI("ind_ab", [128, 18])
    DI("sel2_t", [2, 256])

    if DBG_DUMP:
        dbg_den = nc.dram_tensor("dbg_den", [NCH, 2, TH], F32, kind="ExternalOutput").ap()
        dbg_kv = nc.dram_tensor("dbg_kv", [3, 128, 9 * TH], BF16, kind="ExternalOutput").ap()
        dbg_ind = nc.dram_tensor("dbg_ind", [2, 128, 9], F32, kind="ExternalOutput").ap()
    cc_in = nc.dram_tensor("cc_in", [1024, TOK], BF16).ap()
    cc_out = nc.dram_tensor("cc_out", [8 * 1024, TOK], BF16).ap()
    q_dram2 = nc.dram_tensor("q_dram2", [NCH, 2, 128, TH], BF16).ap()
    ao_dram = nc.dram_tensor("ao_dram", [NCH, 128, TOK], BF16).ap()

    kr_sb = av(L0R, 4 * TOK)
    ckv_sb = av(L0R + 8192, 4 * TOK)
    cosq_sb = av(L0R + 16384, TOK, F32)
    sinq_sb = av(L0R + 20480, TOK, F32)
    cosk_sb = av(L0R + 24576, TOK, F32)
    sink_sb = av(L0R + 28672, TOK, F32)
    qrot_bf = cbu[0]
    scrA = av(L0R + 8192, TOK)
    scrB = av(L0R + 10240, TOK)
    gslot = [av(k * 8192, 8 * TH) for k in range(9)]
    qsel_bufs = [A("qsel1", [128, 9 * TH], BF16), A("qsel2", [128, 9 * TH], BF16)]
    den_rc = A("den_rc", [2, TH], F32)
    sel2_sb = A("sel2_sb", [2, 256], F32)
    kbuilt = av(82944, 9 * TH)
    vT_A = av(92160, 9 * TH)
    vT_B = av(101376, 9 * TH)
    wupk_h = [av(110592, 4 * 128), av(111616, 4 * 128)]
    wupv_h = [av(112640, 4 * 128), av(113664, 4 * 128)]
    ao_sb = av(0, NCH * TOK)
    rq_sb = A("rq_sb", [128, 128], BF16)
    rk_sb = A("rk_sb", [128, 128], BF16)
    sel_sb = [A("sel_e", [128, 128], BF16), A("sel_o", [128, 128], BF16)]
    mtri_sb = av(114688, 2048)
    kvidx_sb = av(118904, 72, U32)
    qidx_sb = av(119192, 9, U32)
    indA_sb = av(118824, 9, F32)
    indB_sb = av(118864, 9, F32)
    indAB_sb = av(118784, 18)
    s_cc = ctx.sem("s_cc")
    s_ao = [ctx.sem("s_ao0"), ctx.sem("s_ao1")]
    s_qd = [ctx.sem("s_qd0"), ctx.sem("s_qd1")]
    s_gt = [ctx.sem(f"s_gt{k}") for k in range(9)]
    s_qs = [ctx.sem("s_qs0"), ctx.sem("s_qs1")]

    def l1attn(xold, xnew, do_stats):
        eset = [[sqb[0][:], sqb[1][:]],
                [evb[0][:].bitcast(BF16), evb[1][:].bitcast(BF16)]]
        def etile(s, kk):
            return eset[s][kk // 2][:, (kk % 2) * TH:(kk % 2 + 1) * TH]
        ctx.dma("gpsimd", rq_sb[:], din["r_q"], s_gm)
        ctx.dma("gpsimd", rk_sb[:], din["r_k"], s_gm)
        ctx.dma("gpsimd", sel_sb[0][:], din["sel_even"], s_gm)
        ctx.dma("gpsimd", sel_sb[1][:], din["sel_odd"], s_gm)
        ctx.dma("gpsimd", mtri_sb, din["m_tri"], s_gm)
        ctx.dma("gpsimd", indAB_sb, din["ind_ab"], s_gm)
        ctx.dma("sync", kvidx_sb, din["kv_idx"], s_sm)
        ctx.dma("sync", qidx_sb, din["q_idx"], s_sm)
        ctx.dma("sync", indA_sb, din["ind_a"], s_sm)
        ctx.dma("sync", indB_sb, din["ind_b"], s_sm)
        ctx.dma("sync", sel2_sb[:], din["sel2_t"], s_sm)
        ctx.wait("sync", s_v)
        ctx.wait("sync", s_mm)
        ctx.dma("sync", cosq_sb, din["cos_q"], s_sm)
        ctx.dma("sync", sinq_sb, din["sin_q"], s_sm)
        ctx.dma("sync", cosk_sb, din["cos_k"], s_sm)
        ctx.dma("sync", sink_sb, din["sin_k"], s_sm)
        ctx.wait("tensor", s_v)
        ctx.wait("tensor", s_sc)
        ctx.wait("tensor", s_gm)
        ctx.wait("gpsimd", s_mm)
        PPa = [ps_main, ps_bc]
        pfree = {}
        kd_ms = {}
        rope_v = {}
        idx = 0
        for which, wt in (("kr", "wkr_t"), ("dk", "wdk_t")):
            for oc in range(4):
                b = idx % 2
                if idx >= 2:
                    ctx.wait("gpsimd", s_mm, kd_ms[idx - 2])
                ctx.dma("gpsimd", wbuf[b][:], din[wt][oc], s_wb[b])
                ctx.wait("tensor", s_wb[b])
                if idx >= 2:
                    sem, val = pfree[idx - 2]
                    ctx.wait("tensor", sem, val)
                def ftp(e, b=b):
                    for th in range(2):
                        sl = slice(th * TH, (th + 1) * TH)
                        for a in range(NCH):
                            last = e.matmul(PPa[b][:, sl], wbuf[b][:, a * 128:(a + 1) * 128],
                                            Xs[:, a * TOK + th * TH: a * TOK + (th + 1) * TH],
                                            start=(a == 0), stop=(a == NCH - 1),
                                            skip_group_check=True)
                    return last
                kd_ms[idx] = ctx.emit("tensor", ftp, s_mm)
                ctx.wait("scalar", s_mm, kd_ms[idx])
                if which == "dk":
                    ev = ctx.emit("scalar", lambda e, oc=oc, b=b: e.copy(
                        out=ckv_sb[:, oc * TOK:(oc + 1) * TOK], in_=PPa[b][:, 0:TOK]), s_sc)
                    pfree[idx] = (s_sc, ev)
                else:
                    if oc >= 1:
                        ctx.wait("scalar", s_v, rope_v[oc - 1])
                    ev = ctx.emit("scalar", lambda e, b=b: e.copy(
                        out=qrot_bf[:], in_=PPa[b][:, 0:TOK]), s_sc)
                    if oc >= 1:
                        ctx.wait("tensor", s_v, rope_v[oc - 1])
                    ctx.wait("tensor", s_sc, ev)
                    def ftk(e):
                        for th in range(2):
                            sl = slice(th * TH, (th + 1) * TH)
                            last = e.matmul(ps_aux[:, sl], rk_sb[:], qrot_bf[:, sl],
                                            start=True, stop=True, skip_group_check=True)
                        return last
                    km = ctx.emit("tensor", ftk, s_mm)
                    ctx.wait("vector", s_mm, km)
                    if oc == 0:
                        ctx.wait("vector", s_sm)
                    def fvk(e, b=b):
                        e.tensor_tensor(sg_sb[:], PPa[b][:, 0:TOK], cosk_sb, OP.mult)
                        return e.tensor_tensor(zsig[:], ps_aux[:, 0:TOK], sink_sb, OP.mult)
                    fv = ctx.emit("vector", fvk, s_v)
                    pfree[idx] = (s_v, fv)
                    ctx.wait("vector", s_v)
                    rope_v[oc] = ctx.emit("vector", lambda e, oc=oc: e.tensor_tensor(
                        kr_sb[:, oc * TOK:(oc + 1) * TOK], sg_sb[:], zsig[:], OP.add), s_v)
                idx += 1
        kr_last_fv = pfree[3]
        ctx.wait("sync", s_v)
        ctx.wait("sync", s_sc)
        ctx.dma("sync", cc_in[0:512, :].rearrange("(oc p) t -> p oc t", p=128),
                kr_sb.rearrange("p (oc t) -> p oc t", oc=4), s_sm)
        ctx.dma("sync", cc_in[512:1024, :].rearrange("(oc p) t -> p oc t", p=128),
                ckv_sb.rearrange("p (oc t) -> p oc t", oc=4), s_sm)
        ctx.wait("gpsimd", s_sm)
        ctx.emit("gpsimd", lambda e: e.collective_compute(
            "AllGather", OP.bypass, replica_groups=[list(range(NCORES))],
            ins=[cc_in], outs=[cc_out]), s_cc)
        qd_rows = q_dram2.rearrange("q c p t -> (q c p) t")
        PQ = [ps_main, ps_bc]
        PR = [ps_aux, ps_rows]
        qrp = [qrot_bf, cbu[1]]
        sgp = [sg_sb, scrA]
        zsp = [zsig, scrB]
        q_ms = {}
        ttq_ms = {}
        for qh in range(NCH):
            b = qh % 2
            if qh >= 2:
                ctx.wait("gpsimd", s_mm, q_ms[qh - 2])
            ctx.dma("gpsimd", wbuf[b][:], din["wq1_t"][qh], s_wb[b])
            ctx.wait("tensor", s_wb[b])
            if qh >= 2:
                ctx.wait("tensor", s_v, ttq_ms[qh - 2])
            else:
                sem, val = pfree[6 + qh]
                ctx.wait("tensor", sem, val)
                if qh == 0:
                    ctx.wait("tensor", s_v, kr_last_fv[1])
            def ftq(e, b=b):
                for th in range(2):
                    sl = slice(th * TH, (th + 1) * TH)
                    for a in range(NCH):
                        last = e.matmul(PQ[b][:, sl], wbuf[b][:, a * 128:(a + 1) * 128],
                                        Xs[:, a * TOK + th * TH: a * TOK + (th + 1) * TH],
                                        start=(a == 0), stop=(a == NCH - 1),
                                        skip_group_check=True)
                return last
            q_ms[qh] = ctx.emit("tensor", ftq, s_mm)
            ctx.wait("scalar", s_mm, q_ms[qh])
            if qh >= 2:
                ctx.wait("scalar", s_qd[b])
            ev = ctx.emit("scalar", lambda e, b=b: e.copy(
                out=qrp[b][:], in_=PQ[b][:, 0:TOK]), s_sc)
            ctx.wait("tensor", s_sc, ev)
            def ftq3(e, b=b):
                for th in range(2):
                    sl = slice(th * TH, (th + 1) * TH)
                    last = e.matmul(PR[b][:, sl], rq_sb[:], qrp[b][:, sl],
                                    start=True, stop=True, skip_group_check=True)
                return last
            rm = ctx.emit("tensor", ftq3, s_mm)
            ctx.wait("vector", s_mm, rm)
            if qh == 1:
                ctx.wait("vector", s_sm)
            def fvq(e, b=b):
                e.tensor_tensor(sgp[b][:], PQ[b][:, 0:TOK], cosq_sb, OP.mult)
                return e.tensor_tensor(zsp[b][:], PR[b][:, 0:TOK], sinq_sb, OP.mult)
            ttq_ms[qh] = ctx.emit("vector", fvq, s_v)
            ctx.wait("vector", s_v)
            ctx.emit("vector", lambda e, b=b: e.tensor_tensor(
                qrp[b][:], sgp[b][:], zsp[b][:], OP.add), s_v)
            ctx.wait("sync", s_v)
            ctx.dma("sync", q_dram2[qh, 0], qrp[b][:, 0:TH], s_qd[b])
            ctx.dma("sync", q_dram2[qh, 1], qrp[b][:, TH:TOK], s_qd[b])
        ctx.wait("gpsimd", s_qd[0])
        ctx.wait("gpsimd", s_qd[1])
        ctx.wait("gpsimd", s_sm)
        for qh0 in range(2):
            for k in range(9):
                ctx.emit("gpsimd", lambda e, k=k, qh0=qh0: e.indirect_dma_start(
                    out=qsel_bufs[qh0][:, k * TH:(k + 1) * TH], out_offset=None,
                    in_=qd_rows,
                    in_offset=bass.IndirectOffsetOnAxis(qidx_sb[:, k:k + 1], 0),
                    element_offset=qh0 * 2 * 128 * TH), s_qs[qh0], 16)
        cc_rows = cc_out.rearrange("r (h t) -> (r h) t", h=2)
        ctx.wait("gpsimd", s_cc)
        ctx.wait("gpsimd", s_sm)
        ctx.wait("gpsimd", s_mm)
        ctx.wait("gpsimd", s_v)
        for k in range(9):
            for dc in range(8):
                ctx.emit("gpsimd", lambda e, k=k, dc=dc: e.indirect_dma_start(
                    out=gslot[k][:, dc * TH:(dc + 1) * TH], out_offset=None,
                    in_=cc_rows,
                    in_offset=bass.IndirectOffsetOnAxis(kvidx_sb[:, k * 8 + dc:k * 8 + dc + 1], 0)),
                    s_gt[k], 16)
        PB = [ps_bc, ps_main]
        kv_ms = {}
        S_last = {}
        exp_ms = []
        av_ms = []
        gi = 0
        fvo_ms = None
        for h in range(8):
            b = h % 2
            if h >= 2:
                ctx.wait("gpsimd", s_mm, kv_ms[h - 2])
            ctx.dma("gpsimd", wupk_h[b].rearrange("p (c d) -> p c d", c=4),
                    din["wupk_t"][h].transpose([1, 0, 2]), s_wb[b])
            ctx.dma("gpsimd", wupv_h[b].rearrange("p (c d) -> p c d", c=4),
                    din["wupv_t"][h].transpose([1, 0, 2]), s_u2[b])
            ctx.wait("tensor", s_wb[b])
            ctx.wait("tensor", s_u2[b])
            if h == 0:
                ctx.wait("tensor", s_gm)
                ctx.wait("tensor", s_sm)
                ctx.wait("tensor", s_sc)
                ctx.wait("tensor", s_v)
            bld_ev = {}
            for k9 in range(9):
                pb = PB[k9 % 2]
                if h == 0:
                    ctx.wait("tensor", s_gt[k9])
                if k9 >= 2:
                    ctx.wait("tensor", s_sc, bld_ev[k9 - 2])
                elif h > 0:
                    if k9 == 0:
                        ctx.wait("tensor", s_sc, exp_ms[-1])
                    else:
                        ctx.wait("tensor", s_v, fvo_ms)
                def ftkb(e, b=b, h=h, k9=k9, pb=pb):
                    for cc in range(4):
                        e.matmul(pb[:, 0:TH], wupk_h[b][:, cc * 128:(cc + 1) * 128],
                                 gslot[k9][:, (4 + cc) * TH:(5 + cc) * TH],
                                 start=(cc == 0), stop=False, skip_group_check=True)
                    e.matmul(pb[:, 0:TH], sel_sb[h % 2][:],
                             gslot[k9][:, (h // 2) * TH:(h // 2 + 1) * TH],
                             start=False, stop=True, skip_group_check=True)
                    for kk in range(4):
                        for cc in range(4):
                            last = e.matmul(
                                pb[:, TH + kk * 128: TH + (kk + 1) * 128],
                                gslot[k9][:, (4 + cc) * TH + kk * 128: (4 + cc) * TH + (kk + 1) * 128],
                                wupv_h[b][:, cc * 128:(cc + 1) * 128],
                                start=(kk == 0 and cc == 0), stop=(kk == 3 and cc == 3),
                                skip_group_check=True)
                    return last
                bm = ctx.emit("tensor", ftkb, s_mm)
                ctx.wait("scalar", s_mm, bm)
                def fkbe(e, k9=k9, pb=pb):
                    e.copy(out=kbuilt[:, k9 * TH:(k9 + 1) * TH], in_=pb[:, 0:TH])
                    e.activation(out=vT_A[:, k9 * TH:(k9 + 1) * TH], in_=pb[:, TH:TOK],
                                 func=AF.Identity, scale=indA_sb[:, k9:k9 + 1])
                    return e.activation(out=vT_B[:, k9 * TH:(k9 + 1) * TH], in_=pb[:, TH:TOK],
                                        func=AF.Identity, scale=indB_sb[:, k9:k9 + 1])
                bld_ev[k9] = ctx.emit("scalar", fkbe, s_sc)
            kv_ms[h] = s_mm.n
            if DBG_DUMP and h == 0:
                ctx.wait("sync", s_sc, bld_ev[8])
                ctx.dma("sync", dbg_kv[0], kbuilt, s_sm)
                ctx.dma("sync", dbg_kv[1], vT_A, s_sm)
                ctx.dma("sync", dbg_kv[2], vT_B, s_sm)
                ctx.dma("sync", dbg_ind[0], indA_sb, s_sm)
                ctx.dma("sync", dbg_ind[1], indB_sb, s_sm)
                ctx.wait("tensor", s_sm)
            for g in range(4):
                qh = 4 * h + g
                gp = g % 2
                if 2 <= qh + 1 < NCH:
                    nb = (qh + 1) % 2
                    if qh >= 1:
                        ctx.wait("gpsimd", s_mm, S_last[qh - 1])
                    for k in range(9):
                        ctx.emit("gpsimd", lambda e, k=k, q1=qh + 1, nb=nb: e.indirect_dma_start(
                            out=qsel_bufs[nb][:, k * TH:(k + 1) * TH], out_offset=None,
                            in_=qd_rows,
                            in_offset=bass.IndirectOffsetOnAxis(qidx_sb[:, k:k + 1], 0),
                            element_offset=q1 * 2 * 128 * TH), s_qs[nb], 16)
                ctx.wait("tensor", s_qs[qh % 2])
                if g == 0:
                    ctx.wait("tensor", s_sc, bld_ev[8])
                if fvo_ms is not None:
                    ctx.wait("tensor", s_v, fvo_ms)
                def emit_av(k9a, gia, mm):
                    s0a = gia % 2
                    if DBG_SERIAL:
                        ctx.wait("tensor", s_sc)
                        ctx.wait("tensor", s_v)
                    if mm is not None:
                        ctx.wait("tensor", s_v, mm)
                    else:
                        ctx.wait("tensor", s_sc, exp_ms[gia])
                    def ftav(e, k9=k9a, gp=gp, s0a=s0a):
                        for kk in range(4):
                            st = (k9 == 0 and kk == 0)
                            sp = (k9 == 8 and kk == 3)
                            et = etile(s0a, kk)
                            e.matmul(ps_main[:, 0:TH],
                                     vT_A[:, k9 * TH + kk * 128: k9 * TH + (kk + 1) * 128],
                                     et, start=st, stop=sp, skip_group_check=True)
                            e.matmul(ps_main[:, TH:TOK],
                                     vT_B[:, k9 * TH + kk * 128: k9 * TH + (kk + 1) * 128],
                                     et, start=st, stop=sp, skip_group_check=True)
                            last = e.matmul(ps_rows[0:2, gp * TH:(gp + 1) * TH],
                                            indAB_sb[:, 2 * k9:2 * k9 + 2],
                                            et, start=st, stop=sp, skip_group_check=True)
                        return last
                    av_ms.append(ctx.emit("tensor", ftav, s_mm))
                mask_ms = None
                prev_mask = None
                for k9 in range(9):
                    if DBG_SERIAL:
                        ctx.wait("tensor", s_sc)
                        ctx.wait("tensor", s_v)
                    if gi >= 1:
                        ctx.wait("tensor", s_sc, exp_ms[gi - 1])
                    def fts(e, k9=k9, qh=qh):
                        qb = qsel_bufs[qh % 2]
                        for kk in range(4):
                            sp = ps_aux if kk < 2 else ps_bc
                            last = e.matmul(sp[:, (kk % 2) * TH:(kk % 2 + 1) * TH],
                                            kbuilt[:, k9 * TH + kk * 128: k9 * TH + (kk + 1) * 128],
                                            qb[:, k9 * TH:(k9 + 1) * TH],
                                            start=True, stop=True, skip_group_check=True)
                        return last
                    sm = ctx.emit("tensor", fts, s_mm)
                    if k9 == 8:
                        S_last[qh] = sm
                    ctx.wait("scalar", s_mm, sm)
                    if DBG_SERIAL:
                        ctx.wait("scalar", s_mm)
                        ctx.wait("scalar", s_v)
                    if gi >= 2:
                        ctx.wait("scalar", s_mm, av_ms[gi - 2])
                    s0 = gi % 2
                    def fexp(e, s0=s0):
                        e.activation(out=eset[s0][0], in_=ps_aux[:], func=AF.Exp)
                        return e.activation(out=eset[s0][1], in_=ps_bc[:], func=AF.Exp)
                    exp_ms.append(ctx.emit("scalar", fexp, s_sc))
                    if k9 >= 7:
                        ctx.wait("vector", s_sc, exp_ms[gi])
                        def fmask(e, s0=s0):
                            e.tensor_tensor(eset[s0][0], eset[s0][0],
                                            mtri_sb[:, 0:1024], OP.mult)
                            return e.tensor_tensor(eset[s0][1], eset[s0][1],
                                                   mtri_sb[:, 1024:2048], OP.mult)
                        mask_ms = ctx.emit("vector", fmask, s_v)
                    if k9 >= 1:
                        emit_av(k9 - 1, gi - 1, prev_mask)
                    prev_mask = mask_ms
                    gi += 1
                emit_av(8, gi - 1, mask_ms)
                ctx.wait("vector", s_mm, av_ms[gi - 1])
                rc = ctx.emit("vector", lambda e, gp=gp: e.reciprocal(
                    den_rc[:], ps_rows[0:2, gp * TH:(gp + 1) * TH]), s_v)
                if DBG_DUMP:
                    ctx.wait("sync", s_v, rc)
                    ctx.dma("sync", dbg_den[qh], den_rc[:], s_sm)
                    ctx.wait("tensor", s_sm)
                ctx.wait("tensor", s_v, rc)
                def ftbc(e):
                    e.matmul(ps_rows[:, 0:TH], sel2_sb[0:2, 0:128], den_rc[:],
                             start=True, stop=True, skip_group_check=True)
                    return e.matmul(ps_rows[:, TH:TOK], sel2_sb[0:2, 128:256], den_rc[:],
                                    start=True, stop=True, skip_group_check=True)
                bc_ms = ctx.emit("tensor", ftbc, s_mm)
                ctx.wait("scalar", s_mm, bc_ms)
                if fvo_ms is not None:
                    ctx.wait("scalar", s_v, fvo_ms)
                evn = ctx.emit("scalar", lambda e: e.copy(
                    out=rs_bc[:], in_=ps_rows[:, 0:TOK]), s_sc)
                ctx.wait("vector", s_sc, evn)
                ctx.wait("vector", s_ao[gp])
                if h == 0 and g < 2:
                    ctx.wait("vector", s_qd[gp])
                def fvo(e, gp=gp):
                    e.tensor_tensor(qrp[gp][:, 0:TH], ps_main[:, 0:TH], rs_bc[:, 0:TH],
                                    OP.mult)
                    return e.tensor_tensor(qrp[gp][:, TH:TOK], ps_main[:, TH:TOK],
                                           rs_bc[:, TH:TOK], OP.mult)
                fvo_ms = ctx.emit("vector", fvo, s_v)
                ctx.wait("sync", s_v, fvo_ms)
                ctx.dma("sync", ao_dram[qh], qrp[gp][:], s_ao[gp])
        ctx.wait("sync", s_ao[0])
        ctx.wait("sync", s_ao[1])
        ctx.wait("sync", s_mm)
        ctx.wait("sync", s_sc)
        for a in range(NCH):
            ctx.dma("sync", ao_sb[:, a * TOK:(a + 1) * TOK], ao_dram[a], s_sm)
        ctx.wait("gpsimd", s_mm)
        residual_begin(4)
        wo_ms = {}
        for j in range(NCH):
            b = j % 2
            pj = ps_main if j % 2 == 0 else ps_aux
            if j >= 2:
                ctx.wait("gpsimd", s_mm, wo_ms[j - 2])
            ctx.dma("gpsimd", wbuf[b][:], din["wo_t"][j], s_wb[b])
            ctx.wait("tensor", s_wb[b])
            if j == 0:
                ctx.wait("tensor", s_sm)
            for th in range(2):
                psl = slice(th * TH, (th + 1) * TH)
                residual_mmwait()
                def ftm(e, b=b, th=th, psl=psl, pj=pj):
                    for a in range(NCH):
                        last = e.matmul(pj[:, psl], wbuf[b][:, a * 128:(a + 1) * 128],
                                        ao_sb[:, a * TOK + th * TH: a * TOK + (th + 1) * TH],
                                        start=(a == 0), stop=(a == NCH - 1))
                    return last
                wo_ms[j] = ctx.emit("tensor", ftm, s_mm)
                residual_chunk(j, th, pj[:, psl], 1.0, xold, xnew, do_stats)
        residual_end(do_stats)

    streams = [x0, xa, xb, xa, xb, xa]
    builders = ["ffn0", "attn0", "ffn0", "ffn1", "attn1", "ffn1"][:n_sub]
    stats_initial(x0)
    for i, kind in enumerate(builders):
        xold = streams[i]
        xnew = y_out if i == n_sub - 1 else streams[i + 1]
        do_stats = i != n_sub - 1
        ln_finalize()
        apply_pass(xold)
        if kind == "ffn0":
            l0ffn(xold, xnew, do_stats)
        elif kind == "attn0":
            l0attn(xold, xnew, do_stats)
        elif kind == "ffn1":
            l1ffn(xold, xnew, do_stats)
        elif kind == "attn1":
            l1attn(xold, xnew, do_stats)
        else:
            raise NotImplementedError(kind)

    ctx.wait("sync", s_stp[0])
    ctx.wait("sync", s_stp[1])
    ctx.run_block()
    stack.close()
    return nc



def _host_prep(inputs):
    x = np.ascontiguousarray(inputs["L0_attn_x"])
    base = {}
    base["wb_t"] = tile_po(np.ascontiguousarray(inputs["L0_ffn_W_base"].T))
    A_ = inputs["L0_ffn_lora_A"]
    acat_T = np.ascontiguousarray(A_.transpose(2, 0, 1).reshape(D, 128))
    base["acat_t"] = np.ascontiguousarray(
        acat_T.reshape(NCH, 128, 128).transpose(1, 0, 2).reshape(128, NCH * 128))
    B_ = inputs["L0_ffn_lora_B"]
    base["bcat"] = np.ascontiguousarray(B_.transpose(0, 2, 1).reshape(128, D))
    rwT = np.ascontiguousarray(inputs["L0_ffn_router_W"].T)
    base["rwt"] = np.ascontiguousarray(
        rwT.reshape(NCH, 128, 16).transpose(1, 0, 2).reshape(128, NCH * 16))
    base["b_cols0"] = np.ascontiguousarray(inputs["L0_ffn_b_base"].reshape(NCH, 128).T)
    base["rb_row"] = inputs["L0_ffn_router_b"].reshape(1, 16)
    base["prior_col"] = inputs["L0_ffn_domain_prior"].reshape(16, 1)
    base["active_row"] = np.ascontiguousarray(inputs["L0_ffn_active"].reshape(1, 4))
    base["active_col"] = np.ascontiguousarray(inputs["L0_ffn_active"].reshape(4, 1))
    base["iota_e8_4"] = np.broadcast_to((np.arange(128) // 8).astype(np.float32), (4, 128)).copy()
    base["iota16_col"] = np.arange(16, dtype=np.float32).reshape(16, 1)
    base["wq_perm2"] = np.ascontiguousarray(inputs["L0_attn_Wq"].transpose(2, 1, 0))
    base["wk_t"] = tile_po(inputs["L0_attn_Wk"][:, :, 0]).reshape(128, D)
    base["wv_t"] = tile_po(inputs["L0_attn_Wv"][:, :, 0]).reshape(128, D)
    base["bq_nat"] = np.ascontiguousarray(inputs["L0_attn_bq"])
    base["bk_col"] = np.ascontiguousarray(inputs["L0_attn_bk"])
    base["bv_col"] = np.ascontiguousarray(inputs["L0_attn_bv"])
    base["ident128"] = np.eye(128, dtype=np.float32)
    ch = np.arange(D)
    base["expand_full"] = (np.arange(128)[:, None] == (ch[None, :] // 32)).astype(np.float32)

    base["u1_t"] = np.ascontiguousarray(inputs["L1_ffn_U1"].reshape(NCH, 128, 1024))
    base["v1_t"] = tile_po(inputs["L1_ffn_V1"])
    base["u2_t"] = np.ascontiguousarray(inputs["L1_ffn_U2"].reshape(86, 128, 1024))
    base["v2_t"] = tile_po(inputs["L1_ffn_V2"])
    b1 = inputs["L1_ffn_b1"]
    base["b1g_cols"] = np.ascontiguousarray(b1[:11008].reshape(86, 128).T)
    base["b1u_cols"] = np.ascontiguousarray(b1[11008:].reshape(86, 128).T)
    base["b2_cols"] = np.ascontiguousarray(inputs["L1_ffn_b2"].reshape(NCH, 128).T)

    base["wq1_t"] = tile_po(np.ascontiguousarray(inputs["L1_attn_Wq"].T))
    base["wkr_t"] = tile_po(np.ascontiguousarray(inputs["L1_attn_Wkr"].T))
    base["wdk_t"] = tile_po(np.ascontiguousarray(inputs["L1_attn_Wdk"].T))
    base["wo_t"] = tile_po(np.ascontiguousarray(inputs["L1_attn_Wo"].T))
    Wupk = inputs["L1_attn_Wupk"]
    arr = np.zeros((512, 1024), np.float32)
    for hh in range(8):
        for jj in range(64):
            dim = 32 + jj if jj < 32 else 96 + (jj - 32)
            arr[:, hh * 128 + dim] = Wupk[64 * hh + jj, :]
    base["wupk_t"] = np.ascontiguousarray(
        arr.reshape(4, 128, 8, 128).transpose(2, 0, 1, 3))
    WupvT = np.ascontiguousarray(inputs["L1_attn_Wupv"].T)
    base["wupv_t"] = np.ascontiguousarray(
        WupvT.reshape(4, 128, 8, 128).transpose(2, 0, 1, 3))
    rq = np.zeros((128, 128), np.float32)
    for i in range(64):
        rq[64 + i, i] = -1.0
        rq[i, 64 + i] = 1.0
    base["r_q"] = rq
    rk = np.zeros((128, 128), np.float32)
    for blk in range(2):
        o = 64 * blk
        for i in range(32):
            rk[o + 32 + i, o + i] = -1.0
            rk[o + i, o + 32 + i] = 1.0
    base["r_k"] = rk
    for par in range(2):
        sel = np.zeros((128, 128), np.float32)
        for jj in range(64):
            dim = jj if jj < 32 else 64 + (jj - 32)
            sel[64 * par + jj, dim] = 1.0
        base["sel_even" if par == 0 else "sel_odd"] = sel
    mt = np.zeros((128, 4, 512), np.float32)
    for kk in range(4):
        mt[:, kk, :] = (np.arange(128)[:, None] + kk * 128 <= np.arange(512)[None, :])
    base["m_tri"] = np.ascontiguousarray(mt.reshape(128, 2048))
    sel2 = np.zeros((2, 256), np.float32)
    sel2[0, 0:128] = 1.0
    sel2[1, 128:256] = 1.0
    base["sel2_t"] = sel2
    inv_freq = 1.0 / 10000.0 ** (np.arange(0, 128, 2, dtype=np.float32) / 128)
    pos_all = np.arange(S, dtype=np.float32)
    fr = np.outer(pos_all, inv_freq)
    emb = np.concatenate([fr, fr], axis=-1)
    cos_t, sin_t = np.cos(emb), np.sin(emb)
    rs128 = np.float32(1.0 / np.sqrt(np.float32(128.0)))

    per_core = []
    for c in range(NCORES):
        m = dict(base)
        sh = np.concatenate([x[0, 512 * c:512 * (c + 1), :],
                             x[1, 512 * (7 - c):512 * (8 - c), :]], axis=0)
        m["x_shard"] = np.ascontiguousarray(sh.T)
        posq = np.concatenate([pos_all[512 * c:512 * (c + 1)],
                               pos_all[512 * (7 - c):512 * (8 - c)]])
        m["cos_q"] = np.ascontiguousarray(cos_t[posq.astype(int)].T * rs128)
        m["sin_q"] = np.ascontiguousarray(sin_t[posq.astype(int)].T * rs128)
        jrow = np.arange(128) % 64
        fidx = np.where(jrow < 32, jrow, jrow - 32)
        hd = np.where(jrow < 32, jrow, 64 + (jrow - 32))
        m["cos_k"] = np.ascontiguousarray(cos_t[posq.astype(int)][:, hd].T)
        m["sin_k"] = np.ascontiguousarray(sin_t[posq.astype(int)][:, hd].T)
        slots = [(r, 0) for r in range(c)] + [(r, 1) for r in range(7, c, -1)]
        slots += [(c, 0), (c, 1)]
        assert len(slots) == 9
        cn = np.array([hhf for r, hhf in slots], dtype=np.int64)
        kvidx = np.zeros((128, 72), np.uint32)
        for kq, (r, hhf) in enumerate(slots):
            for dc in range(8):
                kvidx[:, kq * 8 + dc] = (r * 1024 + dc * 128 + np.arange(128)) * 2 + hhf
        m["kv_idx"] = kvidx
        m["q_idx"] = (cn[None, :] * 128 + np.arange(128)[:, None]).astype(np.uint32)
        ia = (cn == 0).astype(np.float32)
        ib = (cn == 1).astype(np.float32)
        m["ind_a"] = np.broadcast_to(ia, (128, 9)).copy()
        m["ind_b"] = np.broadcast_to(ib, (128, 9)).copy()
        iab = np.zeros((128, 18), np.float32)
        iab[:, 0::2] = ia
        iab[:, 1::2] = ib
        m["ind_ab"] = iab
        per_core.append(m)
    return per_core


def _host_gather(results):
    out = np.empty((Bsz, S, D), dtype=np.float32)
    for c in range(NCORES):
        t = results[c]["y_shard"].T
        out[0, 512 * c:512 * (c + 1), :] = t[0:512]
        out[1, 512 * (7 - c):512 * (8 - c), :] = t[512:1024]
    return out


def kernel(**inputs):
    per_core = _host_prep(inputs)
    nc = build_program(6)
    res = run_bass_kernel_spmd(nc, per_core, core_ids=list(range(NCORES)))
    return _host_gather(res.results)
